# version 26
# baseline (speedup 1.0000x reference)
import numpy as np
import ml_dtypes
import concourse.bass as bass
import concourse.mybir as mybir
from concourse.bass_utils import run_bass_kernel_spmd
from contextlib import ExitStack

F32 = mybir.dt.float32
BF16 = mybir.dt.bfloat16
AX = mybir.AxisListType.X
AF = mybir.ActivationFunctionType
ALU = mybir.AluOpType
BF = ml_dtypes.bfloat16

DEBUG = False

NCORES = 8
NSEQ = 2
T = 2048
E = 2048
H = 16
HD = 128
NTG = 16
NEC = 16
NQC = 4
EXPB = -30.0
LNE = float(-0.5 * np.log(2048.0))
EPS = 1e-5


class _Sem:
    __slots__ = ("h", "n")
    def __init__(self, h):
        self.h = h
        self.n = 0


class _Gate:
    __slots__ = ("w",)
    def __init__(self):
        self.w = None
    def waits(self):
        return [self.w] if self.w is not None else []
    def set(self, sem, cnt):
        self.w = (sem, cnt)


class _Prog:
    def __init__(self, nc, ctx):
        self.nc = nc
        self.ops = []
        self.sv = _Sem(ctx.enter_context(nc.semaphore("sv")))
        self.sa = _Sem(ctx.enter_context(nc.semaphore("sa")))
        self.sp = _Sem(ctx.enter_context(nc.semaphore("sp")))
        self.sy = _Sem(ctx.enter_context(nc.semaphore("sy")))
        self.sg = _Sem(ctx.enter_context(nc.semaphore("sg")))
        self.syx = [_Sem(ctx.enter_context(nc.semaphore("syx0"))),
                    _Sem(ctx.enter_context(nc.semaphore("syx1")))]
        self.syh = [_Sem(ctx.enter_context(nc.semaphore("syh0"))),
                    _Sem(ctx.enter_context(nc.semaphore("syh1")))]
        self.syd = _Sem(ctx.enter_context(nc.semaphore("syd")))
        self.syw = _Sem(ctx.enter_context(nc.semaphore("syw")))

    def _op(self, eng, fn, waits, incs):
        self.ops.append((eng, fn, [w for w in waits if w is not None], incs))

    def vop(self, fn, waits=()):
        w = [(self.sv, self.sv.n)] + list(waits)
        self._op("vector", fn, w, [(self.sv, 1)])
        self.sv.n += 1
        return self.sv.n

    def aop(self, fn, waits=(), chain=False):
        w = list(waits)
        if chain:
            w.append((self.sa, self.sa.n))
        self._op("scalar", fn, w, [(self.sa, 1)])
        self.sa.n += 1
        return self.sa.n

    def pe(self, fn, waits=(), inc=False):
        incs = []
        if inc:
            incs = [(self.sp, 1)]
        self._op("tensor", fn, list(waits), incs)
        if inc:
            self.sp.n += 1
            return self.sp.n
        return None

    def gop(self, fn, waits=()):
        w = [(self.sg, self.sg.n)] + list(waits)
        self._op("gpsimd", fn, w, [(self.sg, 1)])
        self.sg.n += 1
        return self.sg.n

    def dma(self, q, out, in_, waits=(), sem=None):
        if sem is not None:
            s = sem
        else:
            s = self.sy if q == "sync" else self.sg
        def fn(e, out=out, in_=in_):
            return e.dma_start(out=out, in_=in_)
        self._op(q, fn, list(waits), [(s, 16)])
        s.n += 16
        return s.n

    def raw(self, eng, fn, waits=()):
        self._op(eng, fn, list(waits), [])

    def flush(self):
        ops = self.ops
        self.ops = []
        if not ops:
            return
        order = ["sync", "gpsimd", "tensor", "vector", "scalar"]
        present = [e for e in order if any(o[0] == e for o in ops)]
        with self.nc.Block() as block:
            for ename in present:
                sub = [o for o in ops if o[0] == ename]
                def body(engine, sub=sub):
                    for _, fn, waits, incs in sub:
                        for sem, cnt in waits:
                            engine.wait_ge(sem.h, cnt)
                        inst = fn(engine)
                        for sem, k in incs:
                            inst.then_inc(sem.h, k)
                getattr(block, ename)(body)


def _mm(out, lhsT, rhs, start, stop):
    def fn(e, out=out, lhsT=lhsT, rhs=rhs, start=start, stop=stop):
        return e.matmul(out, lhsT, rhs, start=start, stop=stop)
    return fn


def _tr(out, in_, ident):
    def fn(e, out=out, in_=in_, ident=ident):
        return e.transpose(out, in_, ident)
    return fn


def build_program():
    nc = bass.Bass("TRN2", target_bir_lowering=False)

    d_x = nc.dram_tensor("x", [NSEQ, T, E], F32, kind="ExternalInput").ap()
    d_wqk = nc.dram_tensor("wqk", [32, 128, 2048], F32, kind="ExternalInput").ap()
    d_wv = nc.dram_tensor("wv", [16, 128, 2048], F32, kind="ExternalInput").ap()
    d_wp = nc.dram_tensor("wp", [16, 128, 2048], F32, kind="ExternalInput").ap()
    d_bqk = nc.dram_tensor("bqk", [32, 128], F32, kind="ExternalInput").ap()
    d_bv = nc.dram_tensor("bv", [1, 2048], F32, kind="ExternalInput").ap()
    d_bp = nc.dram_tensor("bp", [1, 2048], F32, kind="ExternalInput").ap()
    d_rb = nc.dram_tensor("rb", [1, 64], F32, kind="ExternalInput").ap()
    d_rwt = nc.dram_tensor("rwt", [16, 128, 64], F32, kind="ExternalInput").ap()
    d_eb = nc.dram_tensor("eb", [64, 2048], F32, kind="ExternalInput").ap()
    d_id = nc.dram_tensor("identf", [128, 128], F32, kind="ExternalInput").ap()
    d_mask = nc.dram_tensor("maskbig", [128, 1024], BF16, kind="ExternalInput").ap()
    d_idb = nc.dram_tensor("identb", [128, 128], BF16, kind="ExternalInput").ap()
    d_y = nc.dram_tensor("y", [NSEQ, T, E], F32, kind="ExternalOutput").ap()
    _sk = "ExternalOutput" if DEBUG else "Internal"
    d_qsc = nc.dram_tensor("q_sc", [NSEQ, 16, 128, 2048], BF16, kind=_sk).ap()
    d_ksc = nc.dram_tensor("k_sc", [NSEQ, 16, 128, 2048], BF16, kind=_sk).ap()
    d_vsc = nc.dram_tensor("v_sc", [NSEQ, H, 128, NTG, 128], BF16, kind=_sk).ap()
    if DEBUG:
        d_dbgA = nc.dram_tensor("dbg_h1T", [128, 32768], BF16, kind="ExternalOutput").ap()
        d_dbgC = nc.dram_tensor("dbg_OT", [128, 32768], BF16, kind="ExternalOutput").ap()

    ctx = ExitStack()
    sb = lambda n, s, d: ctx.enter_context(nc.sbuf_tensor(n, s, d))
    psa = lambda n, s, d: ctx.enter_context(nc.psum_tensor(n, s, d))

    big = sb("big", [128, 32768], BF16)
    wbig = sb("wbig", [128, 32768], BF16)
    kqv = [sb(f"kqv{i}", [128, 2048], BF16) for i in range(6)]
    xb2 = [sb(f"xb{i}", [128, 2048], F32) for i in range(2)]
    pf1 = sb("pf1", [128, 2048], F32)
    pf2 = sb("pf2", [128, 2048], F32)
    pf3 = sb("pf3", [128, 2048], F32)
    wqkb = [kqv[0], kqv[1]]
    qkb = [kqv[2], kqv[3]]
    vbb = [kqv[0], kqv[1]]
    pTP = [sb(f"pTP{i}", [128, 1024], BF16) for i in range(2)]
    izb = sb("izb", [128, 512], BF16)
    identf = sb("identf_b", [128, 128], F32)
    cmx = sb("cmx", [128, 2048], BF16)
    maskb = cmx
    onescol = sb("onescol", [128, 1], BF16)
    onesrow = sb("onesrow", [1, 128], BF16)
    cpool = sb("cpool", [128, 4], F32)
    bqkT = sb("bqkT", [128, 32], F32)
    bbc = sb("bbc", [128, 2048], BF16)
    rbb = sb("rbb", [128, 64], BF16)
    rwTb = pTP[0][0:128, 0:1024]
    ebb = cmx
    st24 = sb("st24", [128, 24], F32)
    st24b = sb("st24b", [128, 24], F32)
    asum = sb("asum", [128, 1], F32)
    asA = sb("asA", [128, 1], F32)
    asB = sb("asB", [128, 1], F32)
    sqA = sb("sqA", [128, 1], F32)
    sqB = sb("sqB", [128, 1], F32)
    mm2 = sb("mm2", [128, 1], F32)
    sqsum = sb("sqsum", [128, 1], F32)
    mmean = sb("mmean", [128, 1], F32)
    nvv = sb("nvv", [128, 1], F32)
    st2 = sb("st2", [128, 2], F32)
    vt = sb("vt", [128, 1], F32)
    srt = sb("srt", [128, 1], F32)
    rstd = sb("rstd", [128, 1], F32)
    nmr = sb("nmr", [128, 1], F32)
    nmx = sb("nmx", [128, 1], F32)
    zrow = sb("zrow", [128, 1], F32)
    izr = sb("izr", [128, 1], F32)
    tsc = sb("tsc", [128, 1], F32)
    m8 = sb("m8", [128, 8], F32)
    zap = sb("zap", [128, 64], F32)
    mskf = zap
    lbuf = sb("lbuf", [128, 64], F32)
    pbuf = sb("pbuf", [128, 64], F32)
    idmt = sb("idmt", [128, 128], BF16)
    identb = idmt
    mTb = idmt[0:64, 0:128]
    lnzr = pf1[0:1, 0:512]
    izrow = pTP[1][0:1, 0:512]

    PS = [psa("PS0", [128, 2048], F32), psa("PS1", [128, 2048], F32)]

    p = _Prog(nc, ctx)
    big3 = big[:].rearrange("p (ec t) -> p ec t", ec=16)
    PS3 = [P[:].rearrange("p (j c) -> p j c", j=16) for P in PS]

    g_xb = [_Gate(), _Gate()]
    g_pf = {id(pf1): _Gate(), id(pf2): _Gate(), id(pf3): _Gate()}
    g_kqv = [_Gate() for _ in range(6)]
    g_wqkb = [g_kqv[0], g_kqv[1]]
    g_qkb = [g_kqv[2], g_kqv[3]]
    g_vbb = [g_kqv[0], g_kqv[1]]
    g_ptp = [_Gate(), _Gate()]
    g_ao = [_Gate(), _Gate()]
    g_izrow = _Gate()
    g_rstd = _Gate()
    g_ps = [_Gate(), _Gate()]
    g_zb = _Gate()
    g_izb = _Gate()
    g_vt = _Gate()
    g_srt = _Gate()

    p.raw("gpsimd", lambda e: e.memset(onescol[:], 1.0))
    p.raw("gpsimd", lambda e: e.memset(onesrow[:], 1.0))
    p.raw("gpsimd", lambda e: e.memset(cpool[:, 0:1], EXPB))
    p.raw("gpsimd", lambda e: e.memset(cpool[:, 1:2], LNE))
    p.raw("gpsimd", lambda e: e.memset(cpool[:, 2:3], 0.0))
    p.dma("sync", identf[:], d_id)
    p.dma("gpsimd", rbb[:], bass.AP(d_rb.tensor, 0, [[0, 128], [1, 64]]))
    bqkraw = xb2[0][0:32, 0:128]
    p.dma("sync", bqkraw, d_bqk)
    w_sy0, w_gp0 = p.sy.n, p.sg.n
    p.flush()

    w_tr0 = p.pe(_tr(PS[0][:, 0:32], bqkraw, identf[0:32, 0:32]),
                 waits=[(p.sy, w_sy0)], inc=True)
    p.vop(lambda e: e.tensor_copy(bqkT[:], PS[0][:, 0:32]), waits=[(p.sp, w_tr0)])
    w_setup_v = p.sv.n
    p.flush()

    for eng in ("tensor", "vector", "scalar", "sync", "gpsimd"):
        p.raw(eng, lambda e: e.nop(), waits=[(p.sy, w_sy0), (p.sg, w_gp0), (p.sv, w_setup_v)])
    p.flush()

    for s in range(NSEQ):
        w_bv = p.dma("gpsimd", bbc[:], bass.AP(d_bv.tensor, 0, [[0, 128], [1, 2048]]))
        w_wqk_pre = {}
        for epc in range(2):
            w_wqk_pre[epc] = p.dma("gpsimd", wqkb[epc][:], d_wqk[epc],
                                   waits=g_wqkb[epc].waits(), sem=p.syw)
        w_wv = None
        for ec in range(16):
            w_wv = p.dma("gpsimd", wbig[:, ec * 2048:(ec + 1) * 2048], d_wv[ec])
        pend_ev = None
        for tg in range(NTG):
            par = tg % 2
            pset = PS[par]
            pfh = pf1 if par == 0 else pf2
            w_x = p.dma("sync", xb2[par][:], d_x[s, tg * 128:(tg + 1) * 128, :],
                        waits=g_xb[par].waits(), sem=p.syx[par])
            p.vop(lambda e, par=par: e.bn_stats(st24[:, 0:6], xb2[par][:, 0:512]),
                  waits=[(p.syx[par], w_x)])
            for i in range(1, 4):
                p.vop(lambda e, i=i, par=par: e.bn_stats(st24[:, 6 * i:6 * i + 6],
                                                         xb2[par][:, 512 * i:512 * i + 512]))
            p.vop(lambda e: e.bn_aggr(st2[:], st24[:]))
            p.vop(lambda e: e.tensor_scalar(vt[:], st2[:, 1:2], EPS, None, ALU.add),
                  waits=g_vt.waits())
            p.aop(lambda e: e.activation(srt[:], vt[:], AF.Ln),
                  waits=[(p.sv, p.sv.n)] + g_srt.waits(), chain=True)
            g_vt.set(p.sa, p.sa.n)
            w_rs = p.aop(lambda e: e.activation(rstd[:], srt[:], AF.Exp, bias=0.0, scale=-0.5),
                         waits=g_rstd.waits(), chain=True)
            g_srt.set(p.sa, w_rs)
            w_nm = p.vop(lambda e: e.tensor_scalar(nmr[:], st2[:, 0:1], rstd[:], -1.0,
                                                   ALU.mult, ALU.mult),
                         waits=[(p.sa, w_rs)])
            g_rstd.set(p.sv, w_nm)
            w_h1 = p.aop(lambda e, pfh=pfh, par=par: e.activation(
                pfh[:], xb2[par][:], AF.Identity, bias=nmr[:], scale=rstd[:]),
                waits=[(p.sv, w_nm)] + g_pf[id(pfh)].waits(), chain=True)
            g_rstd.set(p.sa, w_h1)
            g_xb[par].set(p.sa, w_h1)
            w_t = None
            for j in range(16):
                w_t = p.pe(_tr(pset[:, j * 128:(j + 1) * 128], pfh[:, j * 128:(j + 1) * 128],
                               identf[:]),
                           waits=([(p.sa, w_h1)] + g_ps[par].waits()) if j == 0 else (),
                           inc=(j == 15))
            g_pf[id(pfh)].set(p.sp, w_t)
            if pend_ev is not None:
                pend_ev()
            def pend_ev(par=par, tg=tg, w_t=w_t):
                w_ev = p.aop(lambda e, par=par, tg=tg: e.copy(
                    big3[:, :, tg * 128:(tg + 1) * 128], PS3[par]),
                    waits=[(p.sp, w_t)], chain=True)
                g_ps[par].set(p.sa, w_ev)
        pend_ev()
        p.flush()

        if DEBUG and s == 0:
            p.dma("sync", d_dbgA, big[:])
        for epc in range(32):
            par = epc % 2
            pset = PS[par]
            if epc < 2:
                w_w = w_wqk_pre[epc]
            else:
                w_w = p.dma("gpsimd", wqkb[par][:], d_wqk[epc], waits=g_wqkb[par].waits(),
                            sem=p.syw)
            w_mm = None
            for ec in range(16):
                for g in range(4):
                    w_mm = p.pe(_mm(pset[:, g * 512:(g + 1) * 512],
                                    wqkb[par][:, ec * 128:(ec + 1) * 128],
                                    big[:, ec * 2048 + g * 512: ec * 2048 + g * 512 + 512],
                                    start=(ec == 0), stop=(ec == 15)),
                                waits=([(p.syw, w_w)] + g_ps[par].waits()) if (ec == 0 and g == 0) else (),
                                inc=(ec == 15 and g == 3))
            g_wqkb[par].set(p.sp, w_mm)
            w_evl = None
            for g in range(4):
                w_evl = p.vop(lambda e, par=par, g=g, epc=epc: e.tensor_scalar(
                    qkb[par][:, g * 512:(g + 1) * 512], PS[par][:, g * 512:(g + 1) * 512],
                    bqkT[:, epc:epc + 1], None, ALU.add),
                    waits=([(p.sp, w_mm)] + g_qkb[par].waits()) if g == 0 else ())
            g_ps[par].set(p.sv, w_evl)
            dst = d_qsc[s, epc] if epc < 16 else d_ksc[s, epc - 16]
            w_sp = p.dma("sync", dst, qkb[par][:], waits=[(p.sv, w_evl)])
            g_qkb[par].set(p.sy, w_sp)
        for tg in range(NTG):
            par = tg % 2
            pset = PS[par]
            w_mm = None
            for ec in range(16):
                for g in range(4):
                    w_mm = p.pe(_mm(pset[:, g * 512:(g + 1) * 512],
                                    big[:, ec * 2048 + tg * 128: ec * 2048 + tg * 128 + 128],
                                    wbig[:, ec * 2048 + g * 512: ec * 2048 + g * 512 + 512],
                                    start=(ec == 0), stop=(ec == 15)),
                                waits=([(p.sg, w_wv)] + g_ps[par].waits()) if (ec == 0 and g == 0) else (),
                                inc=(ec == 15 and g == 3))
            w_evl = None
            for g in range(4):
                w_evl = p.vop(lambda e, par=par, g=g: e.tensor_tensor(
                    vbb[par][:, g * 512:(g + 1) * 512], PS[par][:, g * 512:(g + 1) * 512],
                    bbc[:, g * 512:(g + 1) * 512], ALU.add),
                    waits=([(p.sp, w_mm), (p.sg, w_wv)] + g_vbb[par].waits()) if g == 0 else ())
            g_ps[par].set(p.sv, w_evl)
            w_sp = p.dma("gpsimd",
                         d_vsc[s, :, :, tg, :].rearrange("h t d -> t h d"),
                         vbb[par][:].rearrange("t (h d) -> t h d", h=16),
                         waits=[(p.sv, w_evl)])
            g_vbb[par].set(p.sg, w_sp)
        p.flush()

        w_mask = p.dma("sync", cmx[0:128, 0:1024], d_mask)
        w_idb = p.dma("sync", identb[:], d_idb)
        w_bp = p.dma("gpsimd", bbc[:], bass.AP(d_bp.tensor, 0, [[0, 128], [1, 2048]]))
        w_wp = None
        for ec in range(16):
            w_wp = p.dma("gpsimd", wbig[:, ec * 2048:(ec + 1) * 2048], d_wp[ec])
        OBv = [PS[1][:, 0:512], PS[1][:, 512:1024]]
        Zv = [PS[1][0:1, 1024:1536], PS[1][0:1, 1536:2048]]
        ZBC = PS[0][:, 0:512]
        g_sbp = [_Gate(), _Gate()]
        g_ob = [_Gate(), _Gate()]
        g_zv = [_Gate(), _Gate()]
        pend = None
        for h in range(H):
            hp = h % 2
            kT, qT, vh = kqv[2 * hp], kqv[2 * hp + 1], kqv[4 + hp]
            gw = g_kqv[2 * hp].waits() + g_kqv[2 * hp + 1].waits() + g_kqv[4 + hp].waits()
            p.dma("sync", kT[:], d_ksc[s, h], waits=gw, sem=p.syh[hp])
            p.dma("sync", qT[:], d_qsc[s, h], sem=p.syh[hp])
            w_in = p.dma("sync", vh[:], d_vsc[s, h].rearrange("t kb d -> t (kb d)"),
                         sem=p.syh[hp])
            w_pvz = None
            pend_pv = None
            for qc in range(NQC):
                nk = 4 * qc + 4
                qcp = qc % 2
                for m in range(nk // 2):
                    par = m % 2
                    if m == 1 and pend is not None:
                        pend()
                        pend = None
                    kb0, kb1 = 2 * m, 2 * m + 1
                    sb0 = PS[0][:, par * 1024: par * 1024 + 512]
                    sb1 = PS[0][:, par * 1024 + 512: par * 1024 + 1024]
                    sbpair = PS[0][:, par * 1024: par * 1024 + 1024]
                    firstw = [(p.syh[hp], w_in), (p.sy, w_idb)] if m == 0 else []
                    roffs = [max(0, 128 * (kb - 4 * qc)) for kb in (kb0, kb1)]
                    maskd = [128 * (kb - 4 * qc) >= 0 for kb in (kb0, kb1)]
                    for half, (sbh, kb) in enumerate(((sb0, kb0), (sb1, kb1))):
                        roff = roffs[half]
                        ww = (firstw + g_sbp[par].waits()) if half == 0 else []
                        if maskd[half] and roff > 0:
                            p.pe(_mm(sbh[:, roff:512], identb[:],
                                     maskb[0:128, 512:1024 - roff], True, False), waits=ww)
                            ww = []
                        elif maskd[half]:
                            p.pe(_mm(sbh, identb[:], maskb[0:128, 512:1024], True, False),
                                 waits=ww)
                            ww = []
                        w_s = p.pe(_mm(sbh[:, roff:512] if roff else sbh,
                                       kT[:, kb * 128:(kb + 1) * 128],
                                       qT[:, qc * 512 + roff:(qc + 1) * 512],
                                       not maskd[half], True),
                                   waits=ww, inc=(half == 1))
                    if roffs[0] == 0 and roffs[1] == 0:
                        w_e = p.aop(lambda e, par=par, sbpair=sbpair: e.activation(
                            pTP[par][:], sbpair, AF.Exp, bias=cpool[:, 0:1], scale=1.0),
                            waits=[(p.sp, w_s)] + g_ptp[par].waits())
                    else:
                        p.aop(lambda e, par=par, r0=roffs[0]: e.activation(
                            pTP[par][:, r0:512], PS[0][:, par * 1024 + r0: par * 1024 + 512],
                            AF.Exp, bias=cpool[:, 0:1], scale=1.0),
                            waits=[(p.sp, w_s)] + g_ptp[par].waits())
                        w_e = p.aop(lambda e, par=par, r1=roffs[1]: e.activation(
                            pTP[par][:, 512 + r1:1024],
                            PS[0][:, par * 1024 + 512 + r1: par * 1024 + 1024],
                            AF.Exp, bias=cpool[:, 0:1], scale=1.0))
                    g_sbp[par].set(p.sa, w_e)
                    rdy = (p.sa, w_e)

                    def pvz(par=par, kb0=kb0, kb1=kb1, rdy=rdy, qcp=qcp, nk=nk,
                            roffs=roffs):
                        nonlocal w_pvz
                        for half, kb in enumerate((kb0, kb1)):
                            r = roffs[half]
                            extra = []
                            if kb == 0:
                                extra = g_ob[qcp].waits() + g_zv[qcp].waits()
                            p.pe(_mm(OBv[qcp][:, r:512] if r else OBv[qcp],
                                     vh[:, kb * 128:(kb + 1) * 128],
                                     pTP[par][:, half * 512 + r:(half + 1) * 512],
                                     kb == 0, kb == nk - 1),
                                 waits=([rdy] + extra) if half == 0 else [])
                        for half, kb in enumerate((kb0, kb1)):
                            r = roffs[half]
                            w_pvz = p.pe(_mm(Zv[qcp][0:1, r:512] if r else Zv[qcp],
                                             onescol[:],
                                             pTP[par][:, half * 512 + r:(half + 1) * 512],
                                             kb == 0, kb == nk - 1),
                                         inc=(kb == kb1))
                        g_ptp[par].set(p.sp, w_pvz)
                    if pend_pv is not None:
                        pend_pv()
                    pend_pv = pvz
                if pend_pv is not None:
                    pend_pv()
                    pend_pv = None

                w_ln = p.aop(lambda e, qcp=qcp: e.activation(lnzr, Zv[qcp], AF.Ln,
                                                             bias=0.0, scale=1.0),
                             waits=[(p.sp, w_pvz)], chain=True)
                g_zv[qcp].set(p.sa, w_ln)
                w_iz = p.aop(lambda e: e.activation(izrow, lnzr, AF.Exp,
                                                    bias=cpool[0:1, 1:2], scale=-1.0),
                             waits=g_izrow.waits(), chain=True)

                def pend(h=h, qc=qc, qcp=qcp, w_iz=w_iz):
                    w_bc = p.pe(_mm(ZBC, onesrow[:], izrow, True, True),
                                waits=[(p.sa, w_iz)] + g_zb.waits() + g_sbp[0].waits(),
                                inc=True)
                    g_izrow.set(p.sp, w_bc)
                    g_ptp[1].set(p.sp, w_bc)
                    w_ze = p.vop(lambda e: e.tensor_copy(izb[:], ZBC),
                                 waits=[(p.sp, w_bc)] + g_izb.waits())
                    g_sbp[0].set(p.sv, w_ze)
                    g_zb.set(p.sv, w_ze)
                    w_oe = p.vop(lambda e, h=h, qc=qc, qcp=qcp: e.tensor_tensor(
                        big[:, h * 2048 + qc * 512: h * 2048 + qc * 512 + 512],
                        OBv[qcp], izb[:], ALU.mult))
                    g_ob[qcp].set(p.sv, w_oe)
                    g_izb.set(p.sv, w_oe)
            for i in (2 * hp, 2 * hp + 1, 4 + hp):
                g_kqv[i].set(p.sp, w_pvz)
        pend()
        pend = None
        p.flush()

        if DEBUG and s == 0:
            p.dma("sync", d_dbgC, big[:])
        w_eb = p.dma("gpsimd", ebb[0:64, 0:2048], d_eb, sem=p.syd)
        w_rwt = p.dma("gpsimd", rwTb.rearrange("p (ec c) -> p ec c", ec=16),
                      d_rwt.rearrange("ec p c -> p ec c"), sem=p.syd)
        aof = [pf1, pf3]
        g_ph = [_Gate(), _Gate()]
        g_ps1 = _Gate()
        pendD_front = None
        pendD_a = None
        boxD = {}
        bqueue = []

        for tg in range(NTG):
            par = tg % 2
            ao = aof[par]
            w_pj = {}
            for hs in range(2):
                for ec in range(16):
                    for g in (2 * hs, 2 * hs + 1):
                        w_pj[hs] = p.pe(_mm(PS[0][:, g * 512:(g + 1) * 512],
                                            big[:, ec * 2048 + tg * 128: ec * 2048 + tg * 128 + 128],
                                            wbig[:, ec * 2048 + g * 512: ec * 2048 + g * 512 + 512],
                                            start=(ec == 0), stop=(ec == 15)),
                                        waits=([(p.sg, w_wp)] + g_ph[hs].waits())
                                              if (ec == 0 and g == 2 * hs) else (),
                                        inc=(ec == 15 and g == 2 * hs + 1))

            if pendD_front is not None:
                pendD_front()
            if len(bqueue) > 1:
                bqueue.pop(0)()
            w_x = p.dma("sync", xb2[par][:], d_x[s, tg * 128:(tg + 1) * 128, :],
                        waits=g_xb[par].waits(), sem=p.syx[par])
            for hs in range(2):
                w_ev = p.vop(lambda e, ao=ao, hs=hs: e.tensor_tensor(
                    ao[:, hs * 1024:(hs + 1) * 1024], PS[0][:, hs * 1024:(hs + 1) * 1024],
                    bbc[:, hs * 1024:(hs + 1) * 1024], ALU.add),
                    waits=[(p.sp, w_pj[hs]), (p.sg, w_wp)] + (g_ao[par].waits() if hs == 0 else []))
                g_ph[hs].set(p.sv, w_ev)
                for i in (2 * hs, 2 * hs + 1):
                    p.vop(lambda e, i=i, ao=ao: e.bn_stats(st24b[:, 6 * i:6 * i + 6],
                                                           ao[:, 512 * i:512 * i + 512]))
            if pendD_a is not None:
                pendD_a()
                pendD_a = None

            def pendD_front(tg=tg, par=par, ao=ao, w_x=w_x, box=boxD):
                p.vop(lambda e: e.bn_aggr(st2[:], st24b[:]))
                p.vop(lambda e: e.tensor_scalar(vt[:], st2[:, 1:2], EPS, None, ALU.add),
                      waits=g_vt.waits())
                p.aop(lambda e: e.activation(srt[:], vt[:], AF.Ln),
                      waits=[(p.sv, p.sv.n)] + g_srt.waits(), chain=True)
                g_vt.set(p.sa, p.sa.n)
                w_rs = p.aop(lambda e: e.activation(rstd[:], srt[:], AF.Exp, bias=0.0,
                                                    scale=-0.5),
                             waits=g_rstd.waits(), chain=True)
                g_srt.set(p.sa, w_rs)
                p.vop(lambda e, ao=ao: e.tensor_scalar(ao[:], ao[:], st2[:, 0:1], rstd[:],
                                                       ALU.subtract, ALU.mult),
                      waits=[(p.sa, w_rs)])
                w_y1 = p.vop(lambda e, par=par, ao=ao: e.tensor_tensor(
                    xb2[par][:], xb2[par][:], ao[:], ALU.add),
                    waits=[(p.syx[par], w_x)])
                g_rstd.set(p.sv, w_y1)
                for i in range(4):
                    p.vop(lambda e, i=i, par=par: e.bn_stats(st24[:, 6 * i:6 * i + 6],
                                                             xb2[par][:, 512 * i:512 * i + 512]))
                p.vop(lambda e: e.bn_aggr(st2[:], st24[:]))
                p.vop(lambda e: e.tensor_scalar(vt[:], st2[:, 1:2], EPS, None, ALU.add),
                      waits=g_vt.waits())
                p.aop(lambda e: e.activation(srt[:], vt[:], AF.Ln),
                      waits=[(p.sv, p.sv.n)] + g_srt.waits(), chain=True)
                g_vt.set(p.sa, p.sa.n)
                w_rs2 = p.aop(lambda e: e.activation(rstd[:], srt[:], AF.Exp, bias=0.0,
                                                     scale=-0.5),
                              waits=g_rstd.waits(), chain=True)
                g_srt.set(p.sa, w_rs2)
                w_h2 = p.vop(lambda e, ao=ao, par=par: e.tensor_scalar(
                    ao[:], xb2[par][:], st2[:, 0:1], rstd[:], ALU.subtract, ALU.mult),
                    waits=[(p.sa, w_rs2)])
                g_rstd.set(p.sv, w_h2)
                box[tg] = {"w_h2": w_h2}

            def pendD_a(tg=tg, par=par, ao=ao, box=boxD):
                bx = box[tg]
                w_t = None
                for j in range(16):
                    w_t = p.pe(_tr(PS[1][:, j * 128:(j + 1) * 128], ao[:, j * 128:(j + 1) * 128],
                                   identf[:]),
                               waits=([(p.sv, bx["w_h2"])] + g_ps1.waits()) if j == 0 else (),
                               inc=(j == 15))
                w_h2T = p.aop(lambda e: e.copy(kqv[0][:], PS[1][:]),
                              waits=[(p.sp, w_t)] + g_kqv[0].waits(), chain=True)
                w_rt = None
                for ec in range(16):
                    w_rt = p.pe(_mm(PS[1][:, 0:64], kqv[0][:, ec * 128:(ec + 1) * 128],
                                    rwTb[:, ec * 64:(ec + 1) * 64], ec == 0, ec == 15),
                                waits=[(p.sa, w_h2T), (p.syd, w_rwt)] if ec == 0 else (),
                                inc=(ec == 15))
                g_kqv[0].set(p.sp, w_rt)
                p.vop(lambda e: e.tensor_tensor(lbuf[:], PS[1][:, 0:64], rbb[:, 0:64], ALU.add),
                      waits=[(p.sp, w_rt)])
                p.vop(lambda e: e.reduce_max(nmx[:], lbuf[:], AX, negate=True))
                w_ex = p.aop(lambda e: e.activation(pbuf[:], lbuf[:], AF.Exp, bias=nmx[:],
                                                    scale=1.0, accum_out=zrow[:]),
                             waits=[(p.sv, p.sv.n)], chain=True)
                p.vop(lambda e: e.reciprocal(izr[:], zrow[:]), waits=[(p.sa, w_ex)])
                p.vop(lambda e: e.tensor_scalar(pbuf[:], pbuf[:], izr[:], None, ALU.mult))
                p.vop(lambda e: e.max(out=m8[:], in_=pbuf[:]))
                p.vop(lambda e: e.match_replace(out=zap[:], in_to_replace=m8[:],
                                                in_values=pbuf[:], imm_value=0.0))
                w_mf = p.vop(lambda e: e.tensor_tensor(zap[:], pbuf[:], zap[:], ALU.subtract))
                w_ts = p.vop(lambda e: e.reduce_sum(tsc[:], m8[:], AX))
                bx["w_mf"] = w_mf
                bx["w_ts"] = w_ts

            def pendD_b(tg=tg, par=par, ao=ao, box=boxD):
                bx = box.pop(tg)
                w_mt = p.pe(_tr(PS[1][0:64, 128:256], mskf[:], identf[:]),
                            waits=[(p.sv, bx["w_mf"])], inc=True)
                w_mtb = p.aop(lambda e: e.copy(mTb, PS[1][0:64, 128:256]),
                              waits=[(p.sp, w_mt)], chain=True)
                w_mo = None
                for g in range(4):
                    w_mo = p.pe(_mm(PS[1][:, g * 512:(g + 1) * 512], mTb,
                                    ebb[0:64, g * 512:(g + 1) * 512], True, True),
                                waits=[(p.sa, w_mtb), (p.sg, w_eb)] if g == 0 else (),
                                inc=(g == 3))
                w_me = None
                for g in range(4):
                    w_me = p.vop(lambda e, g=g, ao=ao: e.scalar_tensor_tensor(
                        ao[:, g * 512:(g + 1) * 512], ao[:, g * 512:(g + 1) * 512], tsc[:],
                        PS[1][:, g * 512:(g + 1) * 512], ALU.mult, ALU.add),
                        waits=[(p.sp, w_mo)] if g == 0 else ())
                g_ps1.set(p.sv, w_me)
                w_sum = p.aop(lambda e, ao=ao: e.activation(ao[:], ao[:], AF.Identity,
                                                            accum_out=asum[:]),
                              waits=[(p.sv, w_me)], chain=True)
                w_sqp = p.aop(lambda e, ao=ao: e.activation(
                    pf2[:], ao[:], AF.Square, accum_out=sqsum[:]),
                    waits=g_pf[id(pf2)].waits(), chain=True)
                p.vop(lambda e: e.tensor_scalar(mmean[:], asum[:], 1.0 / 2048.0, None, ALU.mult),
                      waits=[(p.sa, w_sum)])
                p.vop(lambda e: e.tensor_scalar(vt[:], sqsum[:], 1.0 / 2048.0, EPS,
                                                ALU.mult, ALU.add),
                      waits=[(p.sa, w_sqp)] + g_vt.waits())
                w_nv = p.vop(lambda e: e.scalar_tensor_tensor(nvv[:], mmean[:], mmean[:],
                                                              vt[:], ALU.mult, ALU.subtract))
                p.aop(lambda e: e.activation(srt[:], nvv[:], AF.Ln, bias=0.0, scale=-1.0),
                      waits=[(p.sv, w_nv)] + g_srt.waits(), chain=True)
                g_vt.set(p.sa, p.sa.n)
                w_rs = p.aop(lambda e: e.activation(rstd[:], srt[:], AF.Exp, bias=0.0,
                                                    scale=-0.5),
                             waits=g_rstd.waits(), chain=True)
                g_srt.set(p.sa, w_rs)
                w_nm = p.vop(lambda e: e.tensor_scalar(nmr[:], mmean[:], rstd[:], -1.0,
                                                       ALU.mult, ALU.mult),
                             waits=[(p.sa, w_rs)])
                g_rstd.set(p.sv, w_nm)
                w_t2 = p.aop(lambda e, ao=ao: e.activation(ao[:], ao[:], AF.Identity,
                                                           bias=nmr[:], scale=rstd[:]),
                             waits=[(p.sv, w_nm)], chain=True)
                g_rstd.set(p.sa, w_t2)
                w_o = p.gop(lambda e, par=par, ao=ao: e.tensor_tensor(
                    pf2[:], xb2[par][:], ao[:], ALU.add), waits=[(p.sa, w_t2)])
                g_ao[par].set(p.sg, w_o)
                g_xb[par].set(p.sg, w_o)
                w_od = p.dma("sync", d_y[s, tg * 128:(tg + 1) * 128, :], pf2[:],
                             waits=[(p.sg, w_o)])
                g_pf[id(pf2)].set(p.sy, w_od)
            bqueue.append(pendD_b)
        pendD_front()
        bqueue.pop(0)()
        pendD_a()
        bqueue.pop(0)()
        p.flush()


    p.raw("sync", lambda e: e.nop(), waits=[(p.sy, p.sy.n), (p.sv, p.sv.n),
                                            (p.sa, p.sa.n), (p.syx[0], p.syx[0].n),
                                            (p.syx[1], p.syx[1].n), (p.syh[0], p.syh[0].n),
                                            (p.syh[1], p.syh[1].n)])
    p.raw("gpsimd", lambda e: e.nop(), waits=[(p.sg, p.sg.n), (p.syd, p.syd.n),
                                              (p.syw, p.syw.n)])
    p.flush()

    nc._ctx = ctx
    return nc


def _prep_inputs(L0_attn_x, L0_attn_w_attn, L0_attn_b_attn, L0_attn_w_proj,
                 L0_attn_b_proj, L0_ffn_router_weight, L0_ffn_router_bias,
                 L0_ffn_expert_bias):
    x = np.ascontiguousarray(np.asarray(L0_attn_x, dtype=np.float32))
    wa = np.ascontiguousarray(np.asarray(L0_attn_w_attn, dtype=np.float32))
    ba = np.ascontiguousarray(np.asarray(L0_attn_b_attn, dtype=np.float32))
    wp = np.ascontiguousarray(np.asarray(L0_attn_w_proj, dtype=np.float32))
    bp = np.ascontiguousarray(np.asarray(L0_attn_b_proj, dtype=np.float32))
    rw = np.ascontiguousarray(np.asarray(L0_ffn_router_weight, dtype=np.float32))
    rb = np.ascontiguousarray(np.asarray(L0_ffn_router_bias, dtype=np.float32))
    eb = np.ascontiguousarray(np.asarray(L0_ffn_expert_bias, dtype=np.float32))

    shared = {
        "wqk": np.ascontiguousarray(
            wa[:, :4096].reshape(16, 128, 32, 128).transpose(2, 1, 0, 3)).reshape(32, 128, 2048),
        "wv": np.ascontiguousarray(wa[:, 4096:]).reshape(16, 128, 2048),
        "wp": wp.reshape(16, 128, 2048),
        "bqk": np.ascontiguousarray(ba[:4096].reshape(32, 128)),
        "bv": np.ascontiguousarray(ba[4096:].reshape(1, 2048)),
        "bp": bp.reshape(1, 2048),
        "rb": rb.reshape(1, 64),
        "rwt": np.ascontiguousarray(rw.T).reshape(16, 128, 64),
        "eb": eb,
        "identf": np.eye(128, dtype=np.float32),
        "maskbig": np.where((np.arange(1024)[None, :] - np.arange(128)[:, None]) >= 512,
                            0.0, -50000.0).astype(BF),
        "identb": np.eye(128).astype(BF),
    }
    in_maps = []
    for c in range(NCORES):
        m = dict(shared)
        m["x"] = np.ascontiguousarray(x[c * NSEQ:(c + 1) * NSEQ])
        in_maps.append(m)
    return in_maps


def kernel(**inputs):
    in_maps = _prep_inputs(**inputs)
    nc = build_program()
    res = run_bass_kernel_spmd(nc, in_maps, core_ids=list(range(NCORES)))
    out = np.concatenate([np.asarray(r["y"]) for r in res.results], axis=0)
    return out.astype(np.float32, copy=False)


# version 27
# speedup vs baseline: 1.0243x; 1.0243x over previous
import numpy as np
import ml_dtypes
import concourse.bass as bass
import concourse.mybir as mybir
from concourse.bass_utils import run_bass_kernel_spmd
from contextlib import ExitStack

F32 = mybir.dt.float32
BF16 = mybir.dt.bfloat16
AX = mybir.AxisListType.X
AF = mybir.ActivationFunctionType
ALU = mybir.AluOpType
BF = ml_dtypes.bfloat16

DEBUG = False

NCORES = 8
NSEQ = 2
T = 2048
E = 2048
H = 16
HD = 128
NTG = 16
NEC = 16
NQC = 4
EXPB = -30.0
LNE = float(-0.5 * np.log(2048.0))
EPS = 1e-5


class _Sem:
    __slots__ = ("h", "n")
    def __init__(self, h):
        self.h = h
        self.n = 0


class _Gate:
    __slots__ = ("w",)
    def __init__(self):
        self.w = None
    def waits(self):
        return [self.w] if self.w is not None else []
    def set(self, sem, cnt):
        self.w = (sem, cnt)


class _Prog:
    def __init__(self, nc, ctx):
        self.nc = nc
        self.ops = []
        self.sv = _Sem(ctx.enter_context(nc.semaphore("sv")))
        self.sa = _Sem(ctx.enter_context(nc.semaphore("sa")))
        self.sp = _Sem(ctx.enter_context(nc.semaphore("sp")))
        self.sy = _Sem(ctx.enter_context(nc.semaphore("sy")))
        self.sg = _Sem(ctx.enter_context(nc.semaphore("sg")))
        self.syx = [_Sem(ctx.enter_context(nc.semaphore("syx0"))),
                    _Sem(ctx.enter_context(nc.semaphore("syx1")))]
        self.syh = [_Sem(ctx.enter_context(nc.semaphore("syh0"))),
                    _Sem(ctx.enter_context(nc.semaphore("syh1")))]
        self.syd = _Sem(ctx.enter_context(nc.semaphore("syd")))
        self.syw = [_Sem(ctx.enter_context(nc.semaphore("syw0"))),
                    _Sem(ctx.enter_context(nc.semaphore("syw1")))]
        self.sysp = [_Sem(ctx.enter_context(nc.semaphore("sysp0"))),
                     _Sem(ctx.enter_context(nc.semaphore("sysp1")))]
        self.sysv = [_Sem(ctx.enter_context(nc.semaphore("sysv0"))),
                     _Sem(ctx.enter_context(nc.semaphore("sysv1")))]

    def _op(self, eng, fn, waits, incs):
        self.ops.append((eng, fn, [w for w in waits if w is not None], incs))

    def vop(self, fn, waits=()):
        w = [(self.sv, self.sv.n)] + list(waits)
        self._op("vector", fn, w, [(self.sv, 1)])
        self.sv.n += 1
        return self.sv.n

    def aop(self, fn, waits=(), chain=False):
        w = list(waits)
        if chain:
            w.append((self.sa, self.sa.n))
        self._op("scalar", fn, w, [(self.sa, 1)])
        self.sa.n += 1
        return self.sa.n

    def pe(self, fn, waits=(), inc=False):
        incs = []
        if inc:
            incs = [(self.sp, 1)]
        self._op("tensor", fn, list(waits), incs)
        if inc:
            self.sp.n += 1
            return self.sp.n
        return None

    def gop(self, fn, waits=()):
        w = [(self.sg, self.sg.n)] + list(waits)
        self._op("gpsimd", fn, w, [(self.sg, 1)])
        self.sg.n += 1
        return self.sg.n

    def dma(self, q, out, in_, waits=(), sem=None):
        if sem is not None:
            s = sem
        else:
            s = self.sy if q == "sync" else self.sg
        def fn(e, out=out, in_=in_):
            return e.dma_start(out=out, in_=in_)
        self._op(q, fn, list(waits), [(s, 16)])
        s.n += 16
        return s.n

    def raw(self, eng, fn, waits=()):
        self._op(eng, fn, list(waits), [])

    def flush(self):
        ops = self.ops
        self.ops = []
        if not ops:
            return
        order = ["sync", "gpsimd", "tensor", "vector", "scalar"]
        present = [e for e in order if any(o[0] == e for o in ops)]
        with self.nc.Block() as block:
            for ename in present:
                sub = [o for o in ops if o[0] == ename]
                def body(engine, sub=sub):
                    for _, fn, waits, incs in sub:
                        for sem, cnt in waits:
                            engine.wait_ge(sem.h, cnt)
                        inst = fn(engine)
                        for sem, k in incs:
                            inst.then_inc(sem.h, k)
                getattr(block, ename)(body)


def _mm(out, lhsT, rhs, start, stop):
    def fn(e, out=out, lhsT=lhsT, rhs=rhs, start=start, stop=stop):
        return e.matmul(out, lhsT, rhs, start=start, stop=stop)
    return fn


def _tr(out, in_, ident):
    def fn(e, out=out, in_=in_, ident=ident):
        return e.transpose(out, in_, ident)
    return fn


def build_program():
    nc = bass.Bass("TRN2", target_bir_lowering=False)

    d_x = nc.dram_tensor("x", [NSEQ, T, E], F32, kind="ExternalInput").ap()
    d_wqk = nc.dram_tensor("wqk", [32, 128, 2048], F32, kind="ExternalInput").ap()
    d_wv = nc.dram_tensor("wv", [16, 128, 2048], F32, kind="ExternalInput").ap()
    d_wp = nc.dram_tensor("wp", [16, 128, 2048], F32, kind="ExternalInput").ap()
    d_bqk = nc.dram_tensor("bqk", [32, 128], F32, kind="ExternalInput").ap()
    d_bv = nc.dram_tensor("bv", [1, 2048], F32, kind="ExternalInput").ap()
    d_bp = nc.dram_tensor("bp", [1, 2048], F32, kind="ExternalInput").ap()
    d_rb = nc.dram_tensor("rb", [1, 64], F32, kind="ExternalInput").ap()
    d_rwt = nc.dram_tensor("rwt", [16, 128, 64], F32, kind="ExternalInput").ap()
    d_eb = nc.dram_tensor("eb", [64, 2048], F32, kind="ExternalInput").ap()
    d_id = nc.dram_tensor("identf", [128, 128], F32, kind="ExternalInput").ap()
    d_mask = nc.dram_tensor("maskbig", [128, 1024], BF16, kind="ExternalInput").ap()
    d_idb = nc.dram_tensor("identb", [128, 128], BF16, kind="ExternalInput").ap()
    d_y = nc.dram_tensor("y", [NSEQ, T, E], F32, kind="ExternalOutput").ap()
    _sk = "ExternalOutput" if DEBUG else "Internal"
    d_qsc = nc.dram_tensor("q_sc", [NSEQ, 16, 128, 2048], BF16, kind=_sk).ap()
    d_ksc = nc.dram_tensor("k_sc", [NSEQ, 16, 128, 2048], BF16, kind=_sk).ap()
    d_vsc = nc.dram_tensor("v_sc", [NSEQ, H, 128, NTG, 128], BF16, kind=_sk).ap()
    if DEBUG:
        d_dbgA = nc.dram_tensor("dbg_h1T", [128, 32768], BF16, kind="ExternalOutput").ap()
        d_dbgC = nc.dram_tensor("dbg_OT", [128, 32768], BF16, kind="ExternalOutput").ap()

    ctx = ExitStack()
    sb = lambda n, s, d: ctx.enter_context(nc.sbuf_tensor(n, s, d))
    psa = lambda n, s, d: ctx.enter_context(nc.psum_tensor(n, s, d))

    big = sb("big", [128, 32768], BF16)
    wbig = sb("wbig", [128, 32768], BF16)
    kqv = [sb(f"kqv{i}", [128, 2048], BF16) for i in range(6)]
    xb2 = [sb(f"xb{i}", [128, 2048], F32) for i in range(2)]
    pf1 = sb("pf1", [128, 2048], F32)
    pf2 = sb("pf2", [128, 2048], F32)
    pf3 = sb("pf3", [128, 2048], F32)
    wqkb = [kqv[0], kqv[1]]
    qkb = [kqv[2], kqv[3]]
    vbb = [kqv[0], kqv[1]]
    pTP = [sb(f"pTP{i}", [128, 1024], BF16) for i in range(2)]
    izb = sb("izb", [128, 512], BF16)
    identf = sb("identf_b", [128, 128], F32)
    cmx = sb("cmx", [128, 2048], BF16)
    maskb = cmx
    onescol = sb("onescol", [128, 1], BF16)
    onesrow = sb("onesrow", [1, 128], BF16)
    cpool = sb("cpool", [128, 4], F32)
    bqkT = sb("bqkT", [128, 32], F32)
    bbc = sb("bbc", [128, 2048], BF16)
    rbb = sb("rbb", [128, 64], BF16)
    rwTb = pTP[0][0:128, 0:1024]
    ebb = cmx
    st24 = sb("st24", [128, 24], F32)
    st24b = sb("st24b", [128, 24], F32)
    asum = sb("asum", [128, 1], F32)
    asA = sb("asA", [128, 1], F32)
    asB = sb("asB", [128, 1], F32)
    sqA = sb("sqA", [128, 1], F32)
    sqB = sb("sqB", [128, 1], F32)
    mm2 = sb("mm2", [128, 1], F32)
    sqsum = sb("sqsum", [128, 1], F32)
    mmean = sb("mmean", [128, 1], F32)
    nvv = sb("nvv", [128, 1], F32)
    st2 = sb("st2", [128, 2], F32)
    vt = sb("vt", [128, 1], F32)
    srt = sb("srt", [128, 1], F32)
    rstd = sb("rstd", [128, 1], F32)
    nmr = sb("nmr", [128, 1], F32)
    nmx = sb("nmx", [128, 1], F32)
    zrow = sb("zrow", [128, 1], F32)
    izr = sb("izr", [128, 1], F32)
    tsc = sb("tsc", [128, 1], F32)
    m8 = sb("m8", [128, 8], F32)
    zap = sb("zap", [128, 64], F32)
    mskf = zap
    lbuf = sb("lbuf", [128, 64], F32)
    pbuf = sb("pbuf", [128, 64], F32)
    idmt = sb("idmt", [128, 128], BF16)
    identb = idmt
    mTb = idmt[0:64, 0:128]
    lnzr = pf1[0:1, 0:512]
    izrow = pTP[1][0:1, 0:512]

    PS = [psa("PS0", [128, 2048], F32), psa("PS1", [128, 2048], F32)]

    p = _Prog(nc, ctx)
    big3 = big[:].rearrange("p (ec t) -> p ec t", ec=16)
    PS3 = [P[:].rearrange("p (j c) -> p j c", j=16) for P in PS]

    g_xb = [_Gate(), _Gate()]
    g_pf = {id(pf1): _Gate(), id(pf2): _Gate(), id(pf3): _Gate()}
    g_kqv = [_Gate() for _ in range(6)]
    g_wqkb = [g_kqv[0], g_kqv[1]]
    g_qkb = [g_kqv[2], g_kqv[3]]
    g_vbb = [g_kqv[0], g_kqv[1]]
    g_ptp = [_Gate(), _Gate()]
    g_ao = [_Gate(), _Gate()]
    g_izrow = _Gate()
    g_rstd = _Gate()
    g_ps = [_Gate(), _Gate()]
    g_zb = _Gate()
    g_izb = _Gate()
    g_vt = _Gate()
    g_srt = _Gate()

    p.raw("gpsimd", lambda e: e.memset(onescol[:], 1.0))
    p.raw("gpsimd", lambda e: e.memset(onesrow[:], 1.0))
    p.raw("gpsimd", lambda e: e.memset(cpool[:, 0:1], EXPB))
    p.raw("gpsimd", lambda e: e.memset(cpool[:, 1:2], LNE))
    p.raw("gpsimd", lambda e: e.memset(cpool[:, 2:3], 0.0))
    p.dma("sync", identf[:], d_id)
    p.dma("gpsimd", rbb[:], bass.AP(d_rb.tensor, 0, [[0, 128], [1, 64]]))
    bqkraw = xb2[0][0:32, 0:128]
    p.dma("sync", bqkraw, d_bqk)
    w_sy0, w_gp0 = p.sy.n, p.sg.n
    p.flush()

    w_tr0 = p.pe(_tr(PS[0][:, 0:32], bqkraw, identf[0:32, 0:32]),
                 waits=[(p.sy, w_sy0)], inc=True)
    p.vop(lambda e: e.tensor_copy(bqkT[:], PS[0][:, 0:32]), waits=[(p.sp, w_tr0)])
    w_setup_v = p.sv.n
    p.flush()

    for eng in ("tensor", "vector", "scalar", "sync", "gpsimd"):
        p.raw(eng, lambda e: e.nop(), waits=[(p.sy, w_sy0), (p.sg, w_gp0), (p.sv, w_setup_v)])
    p.flush()

    for s in range(NSEQ):
        w_bv = p.dma("gpsimd", bbc[:], bass.AP(d_bv.tensor, 0, [[0, 128], [1, 2048]]))
        w_wqk_pre = {}
        for epc in range(2):
            w_wqk_pre[epc] = p.dma("gpsimd", wqkb[epc][:], d_wqk[epc],
                                   waits=g_wqkb[epc].waits(), sem=p.syw[epc % 2])
        w_wv = None
        for ec in range(16):
            w_wv = p.dma("gpsimd", wbig[:, ec * 2048:(ec + 1) * 2048], d_wv[ec])
        pend_ev = None
        for tg in range(NTG):
            par = tg % 2
            pset = PS[par]
            pfh = pf1 if par == 0 else pf2
            w_x = p.dma("sync", xb2[par][:], d_x[s, tg * 128:(tg + 1) * 128, :],
                        waits=g_xb[par].waits(), sem=p.syx[par])
            p.vop(lambda e, par=par: e.bn_stats(st24[:, 0:6], xb2[par][:, 0:512]),
                  waits=[(p.syx[par], w_x)])
            for i in range(1, 4):
                p.vop(lambda e, i=i, par=par: e.bn_stats(st24[:, 6 * i:6 * i + 6],
                                                         xb2[par][:, 512 * i:512 * i + 512]))
            p.vop(lambda e: e.bn_aggr(st2[:], st24[:]))
            p.vop(lambda e: e.tensor_scalar(vt[:], st2[:, 1:2], EPS, None, ALU.add),
                  waits=g_vt.waits())
            p.aop(lambda e: e.activation(srt[:], vt[:], AF.Ln),
                  waits=[(p.sv, p.sv.n)] + g_srt.waits(), chain=True)
            g_vt.set(p.sa, p.sa.n)
            w_rs = p.aop(lambda e: e.activation(rstd[:], srt[:], AF.Exp, bias=0.0, scale=-0.5),
                         waits=g_rstd.waits(), chain=True)
            g_srt.set(p.sa, w_rs)
            w_nm = p.vop(lambda e: e.tensor_scalar(nmr[:], st2[:, 0:1], rstd[:], -1.0,
                                                   ALU.mult, ALU.mult),
                         waits=[(p.sa, w_rs)])
            g_rstd.set(p.sv, w_nm)
            w_h1 = p.aop(lambda e, pfh=pfh, par=par: e.activation(
                pfh[:], xb2[par][:], AF.Identity, bias=nmr[:], scale=rstd[:]),
                waits=[(p.sv, w_nm)] + g_pf[id(pfh)].waits(), chain=True)
            g_rstd.set(p.sa, w_h1)
            g_xb[par].set(p.sa, w_h1)
            w_t = None
            for j in range(16):
                w_t = p.pe(_tr(pset[:, j * 128:(j + 1) * 128], pfh[:, j * 128:(j + 1) * 128],
                               identf[:]),
                           waits=([(p.sa, w_h1)] + g_ps[par].waits()) if j == 0 else (),
                           inc=(j == 15))
            g_pf[id(pfh)].set(p.sp, w_t)
            if pend_ev is not None:
                pend_ev()
            def pend_ev(par=par, tg=tg, w_t=w_t):
                w_ev = p.aop(lambda e, par=par, tg=tg: e.copy(
                    big3[:, :, tg * 128:(tg + 1) * 128], PS3[par]),
                    waits=[(p.sp, w_t)], chain=True)
                g_ps[par].set(p.sa, w_ev)
        pend_ev()
        p.flush()

        if DEBUG and s == 0:
            p.dma("sync", d_dbgA, big[:])
        for epc in range(32):
            par = epc % 2
            pset = PS[par]
            if epc < 2:
                w_w = w_wqk_pre[epc]
            else:
                w_w = p.dma("gpsimd", wqkb[par][:], d_wqk[epc], waits=g_wqkb[par].waits(),
                            sem=p.syw[par])
            w_mm = None
            for ec in range(16):
                for g in range(4):
                    w_mm = p.pe(_mm(pset[:, g * 512:(g + 1) * 512],
                                    wqkb[par][:, ec * 128:(ec + 1) * 128],
                                    big[:, ec * 2048 + g * 512: ec * 2048 + g * 512 + 512],
                                    start=(ec == 0), stop=(ec == 15)),
                                waits=([(p.syw[par], w_w)] + g_ps[par].waits()) if (ec == 0 and g == 0) else (),
                                inc=(ec == 15 and g == 3))
            g_wqkb[par].set(p.sp, w_mm)
            w_evl = None
            for g in range(4):
                w_evl = p.vop(lambda e, par=par, g=g, epc=epc: e.tensor_scalar(
                    qkb[par][:, g * 512:(g + 1) * 512], PS[par][:, g * 512:(g + 1) * 512],
                    bqkT[:, epc:epc + 1], None, ALU.add),
                    waits=([(p.sp, w_mm)] + g_qkb[par].waits()) if g == 0 else ())
            g_ps[par].set(p.sv, w_evl)
            dst = d_qsc[s, epc] if epc < 16 else d_ksc[s, epc - 16]
            w_sp = p.dma("sync", dst, qkb[par][:], waits=[(p.sv, w_evl)], sem=p.sysp[par])
            g_qkb[par].set(p.sysp[par], w_sp)
        for tg in range(NTG):
            par = tg % 2
            pset = PS[par]
            w_mm = None
            for ec in range(16):
                for g in range(4):
                    w_mm = p.pe(_mm(pset[:, g * 512:(g + 1) * 512],
                                    big[:, ec * 2048 + tg * 128: ec * 2048 + tg * 128 + 128],
                                    wbig[:, ec * 2048 + g * 512: ec * 2048 + g * 512 + 512],
                                    start=(ec == 0), stop=(ec == 15)),
                                waits=([(p.sg, w_wv)] + g_ps[par].waits()) if (ec == 0 and g == 0) else (),
                                inc=(ec == 15 and g == 3))
            w_evl = None
            for g in range(4):
                w_evl = p.vop(lambda e, par=par, g=g: e.tensor_tensor(
                    vbb[par][:, g * 512:(g + 1) * 512], PS[par][:, g * 512:(g + 1) * 512],
                    bbc[:, g * 512:(g + 1) * 512], ALU.add),
                    waits=([(p.sp, w_mm), (p.sg, w_wv)] + g_vbb[par].waits()) if g == 0 else ())
            g_ps[par].set(p.sv, w_evl)
            w_sp = p.dma("gpsimd",
                         d_vsc[s, :, :, tg, :].rearrange("h t d -> t h d"),
                         vbb[par][:].rearrange("t (h d) -> t h d", h=16),
                         waits=[(p.sv, w_evl)], sem=p.sysv[par])
            g_vbb[par].set(p.sysv[par], w_sp)
        p.flush()

        w_mask = p.dma("sync", cmx[0:128, 0:1024], d_mask)
        w_idb = p.dma("sync", identb[:], d_idb)
        w_bp = p.dma("gpsimd", bbc[:], bass.AP(d_bp.tensor, 0, [[0, 128], [1, 2048]]))
        w_wp = None
        for ec in range(16):
            w_wp = p.dma("gpsimd", wbig[:, ec * 2048:(ec + 1) * 2048], d_wp[ec])
        OBv = [PS[1][:, 0:512], PS[1][:, 512:1024]]
        Zv = [PS[1][0:1, 1024:1536], PS[1][0:1, 1536:2048]]
        ZBC = PS[0][:, 0:512]
        g_sbp = [_Gate(), _Gate()]
        g_ob = [_Gate(), _Gate()]
        g_zv = [_Gate(), _Gate()]
        pend = None
        for h in range(H):
            hp = h % 2
            kT, qT, vh = kqv[2 * hp], kqv[2 * hp + 1], kqv[4 + hp]
            gw = g_kqv[2 * hp].waits() + g_kqv[2 * hp + 1].waits() + g_kqv[4 + hp].waits()
            p.dma("sync", kT[:], d_ksc[s, h], waits=gw, sem=p.syh[hp])
            p.dma("sync", qT[:], d_qsc[s, h], sem=p.syh[hp])
            w_in = p.dma("sync", vh[:], d_vsc[s, h].rearrange("t kb d -> t (kb d)"),
                         sem=p.syh[hp])
            w_pvz = None
            pend_pv = None
            for qc in range(NQC):
                nk = 4 * qc + 4
                qcp = qc % 2
                for m in range(nk // 2):
                    par = m % 2
                    if m == 1 and pend is not None:
                        pend()
                        pend = None
                    kb0, kb1 = 2 * m, 2 * m + 1
                    sb0 = PS[0][:, par * 1024: par * 1024 + 512]
                    sb1 = PS[0][:, par * 1024 + 512: par * 1024 + 1024]
                    sbpair = PS[0][:, par * 1024: par * 1024 + 1024]
                    firstw = [(p.syh[hp], w_in), (p.sy, w_idb)] if m == 0 else []
                    roffs = [max(0, 128 * (kb - 4 * qc)) for kb in (kb0, kb1)]
                    maskd = [128 * (kb - 4 * qc) >= 0 for kb in (kb0, kb1)]
                    for half, (sbh, kb) in enumerate(((sb0, kb0), (sb1, kb1))):
                        roff = roffs[half]
                        ww = (firstw + g_sbp[par].waits()) if half == 0 else []
                        if maskd[half] and roff > 0:
                            p.pe(_mm(sbh[:, roff:512], identb[:],
                                     maskb[0:128, 512:1024 - roff], True, False), waits=ww)
                            ww = []
                        elif maskd[half]:
                            p.pe(_mm(sbh, identb[:], maskb[0:128, 512:1024], True, False),
                                 waits=ww)
                            ww = []
                        w_s = p.pe(_mm(sbh[:, roff:512] if roff else sbh,
                                       kT[:, kb * 128:(kb + 1) * 128],
                                       qT[:, qc * 512 + roff:(qc + 1) * 512],
                                       not maskd[half], True),
                                   waits=ww, inc=(half == 1))
                    if roffs[0] == 0 and roffs[1] == 0:
                        w_e = p.aop(lambda e, par=par, sbpair=sbpair: e.activation(
                            pTP[par][:], sbpair, AF.Exp, bias=cpool[:, 0:1], scale=1.0),
                            waits=[(p.sp, w_s)] + g_ptp[par].waits())
                    else:
                        p.aop(lambda e, par=par, r0=roffs[0]: e.activation(
                            pTP[par][:, r0:512], PS[0][:, par * 1024 + r0: par * 1024 + 512],
                            AF.Exp, bias=cpool[:, 0:1], scale=1.0),
                            waits=[(p.sp, w_s)] + g_ptp[par].waits())
                        w_e = p.aop(lambda e, par=par, r1=roffs[1]: e.activation(
                            pTP[par][:, 512 + r1:1024],
                            PS[0][:, par * 1024 + 512 + r1: par * 1024 + 1024],
                            AF.Exp, bias=cpool[:, 0:1], scale=1.0))
                    g_sbp[par].set(p.sa, w_e)
                    rdy = (p.sa, w_e)

                    def pvz(par=par, kb0=kb0, kb1=kb1, rdy=rdy, qcp=qcp, nk=nk,
                            roffs=roffs):
                        nonlocal w_pvz
                        for half, kb in enumerate((kb0, kb1)):
                            r = roffs[half]
                            extra = []
                            if kb == 0:
                                extra = g_ob[qcp].waits() + g_zv[qcp].waits()
                            p.pe(_mm(OBv[qcp][:, r:512] if r else OBv[qcp],
                                     vh[:, kb * 128:(kb + 1) * 128],
                                     pTP[par][:, half * 512 + r:(half + 1) * 512],
                                     kb == 0, kb == nk - 1),
                                 waits=([rdy] + extra) if half == 0 else [])
                        for half, kb in enumerate((kb0, kb1)):
                            r = roffs[half]
                            w_pvz = p.pe(_mm(Zv[qcp][0:1, r:512] if r else Zv[qcp],
                                             onescol[:],
                                             pTP[par][:, half * 512 + r:(half + 1) * 512],
                                             kb == 0, kb == nk - 1),
                                         inc=(kb == kb1))
                        g_ptp[par].set(p.sp, w_pvz)
                    if pend_pv is not None:
                        pend_pv()
                    pend_pv = pvz
                if pend_pv is not None:
                    pend_pv()
                    pend_pv = None

                w_ln = p.aop(lambda e, qcp=qcp: e.activation(lnzr, Zv[qcp], AF.Ln,
                                                             bias=0.0, scale=1.0),
                             waits=[(p.sp, w_pvz)], chain=True)
                g_zv[qcp].set(p.sa, w_ln)
                w_iz = p.aop(lambda e: e.activation(izrow, lnzr, AF.Exp,
                                                    bias=cpool[0:1, 1:2], scale=-1.0),
                             waits=g_izrow.waits(), chain=True)

                def pend(h=h, qc=qc, qcp=qcp, w_iz=w_iz):
                    w_bc = p.pe(_mm(ZBC, onesrow[:], izrow, True, True),
                                waits=[(p.sa, w_iz)] + g_zb.waits() + g_sbp[0].waits(),
                                inc=True)
                    g_izrow.set(p.sp, w_bc)
                    g_ptp[1].set(p.sp, w_bc)
                    w_ze = p.vop(lambda e: e.tensor_copy(izb[:], ZBC),
                                 waits=[(p.sp, w_bc)] + g_izb.waits())
                    g_sbp[0].set(p.sv, w_ze)
                    g_zb.set(p.sv, w_ze)
                    w_oe = p.vop(lambda e, h=h, qc=qc, qcp=qcp: e.tensor_tensor(
                        big[:, h * 2048 + qc * 512: h * 2048 + qc * 512 + 512],
                        OBv[qcp], izb[:], ALU.mult))
                    g_ob[qcp].set(p.sv, w_oe)
                    g_izb.set(p.sv, w_oe)
            for i in (2 * hp, 2 * hp + 1, 4 + hp):
                g_kqv[i].set(p.sp, w_pvz)
        pend()
        pend = None
        p.flush()

        if DEBUG and s == 0:
            p.dma("sync", d_dbgC, big[:])
        w_eb = p.dma("gpsimd", ebb[0:64, 0:2048], d_eb, sem=p.syd)
        w_rwt = p.dma("gpsimd", rwTb.rearrange("p (ec c) -> p ec c", ec=16),
                      d_rwt.rearrange("ec p c -> p ec c"), sem=p.syd)
        aof = [pf1, pf3]
        g_ph = [_Gate(), _Gate()]
        g_ps1 = _Gate()
        pendD_front = None
        pendD_a = None
        boxD = {}
        bqueue = []

        for tg in range(NTG):
            par = tg % 2
            ao = aof[par]
            w_pj = {}
            for hs in range(2):
                for ec in range(16):
                    for g in (2 * hs, 2 * hs + 1):
                        w_pj[hs] = p.pe(_mm(PS[0][:, g * 512:(g + 1) * 512],
                                            big[:, ec * 2048 + tg * 128: ec * 2048 + tg * 128 + 128],
                                            wbig[:, ec * 2048 + g * 512: ec * 2048 + g * 512 + 512],
                                            start=(ec == 0), stop=(ec == 15)),
                                        waits=([(p.sg, w_wp)] + g_ph[hs].waits())
                                              if (ec == 0 and g == 2 * hs) else (),
                                        inc=(ec == 15 and g == 2 * hs + 1))

            if pendD_front is not None:
                pendD_front()
            if len(bqueue) > 1:
                bqueue.pop(0)()
            w_x = p.dma("sync", xb2[par][:], d_x[s, tg * 128:(tg + 1) * 128, :],
                        waits=g_xb[par].waits(), sem=p.syx[par])
            for hs in range(2):
                w_ev = p.vop(lambda e, ao=ao, hs=hs: e.tensor_tensor(
                    ao[:, hs * 1024:(hs + 1) * 1024], PS[0][:, hs * 1024:(hs + 1) * 1024],
                    bbc[:, hs * 1024:(hs + 1) * 1024], ALU.add),
                    waits=[(p.sp, w_pj[hs]), (p.sg, w_wp)] + (g_ao[par].waits() if hs == 0 else []))
                g_ph[hs].set(p.sv, w_ev)
                for i in (2 * hs, 2 * hs + 1):
                    p.vop(lambda e, i=i, ao=ao: e.bn_stats(st24b[:, 6 * i:6 * i + 6],
                                                           ao[:, 512 * i:512 * i + 512]))
            if pendD_a is not None:
                pendD_a()
                pendD_a = None

            def pendD_front(tg=tg, par=par, ao=ao, w_x=w_x, box=boxD):
                p.vop(lambda e: e.bn_aggr(st2[:], st24b[:]))
                p.vop(lambda e: e.tensor_scalar(vt[:], st2[:, 1:2], EPS, None, ALU.add),
                      waits=g_vt.waits())
                p.aop(lambda e: e.activation(srt[:], vt[:], AF.Ln),
                      waits=[(p.sv, p.sv.n)] + g_srt.waits(), chain=True)
                g_vt.set(p.sa, p.sa.n)
                w_rs = p.aop(lambda e: e.activation(rstd[:], srt[:], AF.Exp, bias=0.0,
                                                    scale=-0.5),
                             waits=g_rstd.waits(), chain=True)
                g_srt.set(p.sa, w_rs)
                p.vop(lambda e, ao=ao: e.tensor_scalar(ao[:], ao[:], st2[:, 0:1], rstd[:],
                                                       ALU.subtract, ALU.mult),
                      waits=[(p.sa, w_rs)])
                w_y1 = p.vop(lambda e, par=par, ao=ao: e.tensor_tensor(
                    xb2[par][:], xb2[par][:], ao[:], ALU.add),
                    waits=[(p.syx[par], w_x)])
                g_rstd.set(p.sv, w_y1)
                for i in range(4):
                    p.vop(lambda e, i=i, par=par: e.bn_stats(st24[:, 6 * i:6 * i + 6],
                                                             xb2[par][:, 512 * i:512 * i + 512]))
                p.vop(lambda e: e.bn_aggr(st2[:], st24[:]))
                p.vop(lambda e: e.tensor_scalar(vt[:], st2[:, 1:2], EPS, None, ALU.add),
                      waits=g_vt.waits())
                p.aop(lambda e: e.activation(srt[:], vt[:], AF.Ln),
                      waits=[(p.sv, p.sv.n)] + g_srt.waits(), chain=True)
                g_vt.set(p.sa, p.sa.n)
                w_rs2 = p.aop(lambda e: e.activation(rstd[:], srt[:], AF.Exp, bias=0.0,
                                                     scale=-0.5),
                              waits=g_rstd.waits(), chain=True)
                g_srt.set(p.sa, w_rs2)
                w_h2 = p.vop(lambda e, ao=ao, par=par: e.tensor_scalar(
                    ao[:], xb2[par][:], st2[:, 0:1], rstd[:], ALU.subtract, ALU.mult),
                    waits=[(p.sa, w_rs2)])
                g_rstd.set(p.sv, w_h2)
                box[tg] = {"w_h2": w_h2}

            def pendD_a(tg=tg, par=par, ao=ao, box=boxD):
                bx = box[tg]
                w_t = None
                for j in range(16):
                    w_t = p.pe(_tr(PS[1][:, j * 128:(j + 1) * 128], ao[:, j * 128:(j + 1) * 128],
                                   identf[:]),
                               waits=([(p.sv, bx["w_h2"])] + g_ps1.waits()) if j == 0 else (),
                               inc=(j == 15))
                w_h2T = p.aop(lambda e: e.copy(kqv[0][:], PS[1][:]),
                              waits=[(p.sp, w_t)] + g_kqv[0].waits(), chain=True)
                w_rt = None
                for ec in range(16):
                    w_rt = p.pe(_mm(PS[1][:, 0:64], kqv[0][:, ec * 128:(ec + 1) * 128],
                                    rwTb[:, ec * 64:(ec + 1) * 64], ec == 0, ec == 15),
                                waits=[(p.sa, w_h2T), (p.syd, w_rwt)] if ec == 0 else (),
                                inc=(ec == 15))
                g_kqv[0].set(p.sp, w_rt)
                p.vop(lambda e: e.tensor_tensor(lbuf[:], PS[1][:, 0:64], rbb[:, 0:64], ALU.add),
                      waits=[(p.sp, w_rt)])
                p.vop(lambda e: e.reduce_max(nmx[:], lbuf[:], AX, negate=True))
                w_ex = p.aop(lambda e: e.activation(pbuf[:], lbuf[:], AF.Exp, bias=nmx[:],
                                                    scale=1.0, accum_out=zrow[:]),
                             waits=[(p.sv, p.sv.n)], chain=True)
                p.vop(lambda e: e.reciprocal(izr[:], zrow[:]), waits=[(p.sa, w_ex)])
                p.vop(lambda e: e.tensor_scalar(pbuf[:], pbuf[:], izr[:], None, ALU.mult))
                p.vop(lambda e: e.max(out=m8[:], in_=pbuf[:]))
                p.vop(lambda e: e.match_replace(out=zap[:], in_to_replace=m8[:],
                                                in_values=pbuf[:], imm_value=0.0))
                w_mf = p.vop(lambda e: e.tensor_tensor(zap[:], pbuf[:], zap[:], ALU.subtract))
                w_ts = p.vop(lambda e: e.reduce_sum(tsc[:], m8[:], AX))
                bx["w_mf"] = w_mf
                bx["w_ts"] = w_ts

            def pendD_b(tg=tg, par=par, ao=ao, box=boxD):
                bx = box.pop(tg)
                w_mt = p.pe(_tr(PS[1][0:64, 128:256], mskf[:], identf[:]),
                            waits=[(p.sv, bx["w_mf"])], inc=True)
                w_mtb = p.aop(lambda e: e.copy(mTb, PS[1][0:64, 128:256]),
                              waits=[(p.sp, w_mt)], chain=True)
                w_mo = None
                for g in range(4):
                    w_mo = p.pe(_mm(PS[1][:, g * 512:(g + 1) * 512], mTb,
                                    ebb[0:64, g * 512:(g + 1) * 512], True, True),
                                waits=[(p.sa, w_mtb), (p.sg, w_eb)] if g == 0 else (),
                                inc=(g == 3))
                w_me = None
                for g in range(4):
                    w_me = p.vop(lambda e, g=g, ao=ao: e.scalar_tensor_tensor(
                        ao[:, g * 512:(g + 1) * 512], ao[:, g * 512:(g + 1) * 512], tsc[:],
                        PS[1][:, g * 512:(g + 1) * 512], ALU.mult, ALU.add),
                        waits=[(p.sp, w_mo)] if g == 0 else ())
                g_ps1.set(p.sv, w_me)
                w_sum = p.aop(lambda e, ao=ao: e.activation(ao[:], ao[:], AF.Identity,
                                                            accum_out=asum[:]),
                              waits=[(p.sv, w_me)], chain=True)
                w_sqp = p.aop(lambda e, ao=ao: e.activation(
                    pf2[:], ao[:], AF.Square, accum_out=sqsum[:]),
                    waits=g_pf[id(pf2)].waits(), chain=True)
                p.vop(lambda e: e.tensor_scalar(mmean[:], asum[:], 1.0 / 2048.0, None, ALU.mult),
                      waits=[(p.sa, w_sum)])
                p.vop(lambda e: e.tensor_scalar(vt[:], sqsum[:], 1.0 / 2048.0, EPS,
                                                ALU.mult, ALU.add),
                      waits=[(p.sa, w_sqp)] + g_vt.waits())
                w_nv = p.vop(lambda e: e.scalar_tensor_tensor(nvv[:], mmean[:], mmean[:],
                                                              vt[:], ALU.mult, ALU.subtract))
                p.aop(lambda e: e.activation(srt[:], nvv[:], AF.Ln, bias=0.0, scale=-1.0),
                      waits=[(p.sv, w_nv)] + g_srt.waits(), chain=True)
                g_vt.set(p.sa, p.sa.n)
                w_rs = p.aop(lambda e: e.activation(rstd[:], srt[:], AF.Exp, bias=0.0,
                                                    scale=-0.5),
                             waits=g_rstd.waits(), chain=True)
                g_srt.set(p.sa, w_rs)
                w_nm = p.vop(lambda e: e.tensor_scalar(nmr[:], mmean[:], rstd[:], -1.0,
                                                       ALU.mult, ALU.mult),
                             waits=[(p.sa, w_rs)])
                g_rstd.set(p.sv, w_nm)
                w_t2 = p.aop(lambda e, ao=ao: e.activation(ao[:], ao[:], AF.Identity,
                                                           bias=nmr[:], scale=rstd[:]),
                             waits=[(p.sv, w_nm)], chain=True)
                g_rstd.set(p.sa, w_t2)
                w_o = p.gop(lambda e, par=par, ao=ao: e.tensor_tensor(
                    pf2[:], xb2[par][:], ao[:], ALU.add), waits=[(p.sa, w_t2)])
                g_ao[par].set(p.sg, w_o)
                g_xb[par].set(p.sg, w_o)
                w_od = p.dma("sync", d_y[s, tg * 128:(tg + 1) * 128, :], pf2[:],
                             waits=[(p.sg, w_o)])
                g_pf[id(pf2)].set(p.sy, w_od)
            bqueue.append(pendD_b)
        pendD_front()
        bqueue.pop(0)()
        pendD_a()
        bqueue.pop(0)()
        p.flush()


    p.raw("sync", lambda e: e.nop(), waits=[(p.sy, p.sy.n), (p.sv, p.sv.n),
                                            (p.sa, p.sa.n), (p.syx[0], p.syx[0].n),
                                            (p.syx[1], p.syx[1].n), (p.syh[0], p.syh[0].n),
                                            (p.syh[1], p.syh[1].n), (p.sysp[0], p.sysp[0].n),
                                            (p.sysp[1], p.sysp[1].n)])
    p.raw("gpsimd", lambda e: e.nop(),
          waits=[(p.sg, p.sg.n), (p.syd, p.syd.n), (p.syw[0], p.syw[0].n),
                 (p.syw[1], p.syw[1].n), (p.sysv[0], p.sysv[0].n),
                 (p.sysv[1], p.sysv[1].n)])
    p.flush()

    nc._ctx = ctx
    return nc


def _prep_inputs(L0_attn_x, L0_attn_w_attn, L0_attn_b_attn, L0_attn_w_proj,
                 L0_attn_b_proj, L0_ffn_router_weight, L0_ffn_router_bias,
                 L0_ffn_expert_bias):
    x = np.ascontiguousarray(np.asarray(L0_attn_x, dtype=np.float32))
    wa = np.ascontiguousarray(np.asarray(L0_attn_w_attn, dtype=np.float32))
    ba = np.ascontiguousarray(np.asarray(L0_attn_b_attn, dtype=np.float32))
    wp = np.ascontiguousarray(np.asarray(L0_attn_w_proj, dtype=np.float32))
    bp = np.ascontiguousarray(np.asarray(L0_attn_b_proj, dtype=np.float32))
    rw = np.ascontiguousarray(np.asarray(L0_ffn_router_weight, dtype=np.float32))
    rb = np.ascontiguousarray(np.asarray(L0_ffn_router_bias, dtype=np.float32))
    eb = np.ascontiguousarray(np.asarray(L0_ffn_expert_bias, dtype=np.float32))

    shared = {
        "wqk": np.ascontiguousarray(
            wa[:, :4096].reshape(16, 128, 32, 128).transpose(2, 1, 0, 3)).reshape(32, 128, 2048),
        "wv": np.ascontiguousarray(wa[:, 4096:]).reshape(16, 128, 2048),
        "wp": wp.reshape(16, 128, 2048),
        "bqk": np.ascontiguousarray(ba[:4096].reshape(32, 128)),
        "bv": np.ascontiguousarray(ba[4096:].reshape(1, 2048)),
        "bp": bp.reshape(1, 2048),
        "rb": rb.reshape(1, 64),
        "rwt": np.ascontiguousarray(rw.T).reshape(16, 128, 64),
        "eb": eb,
        "identf": np.eye(128, dtype=np.float32),
        "maskbig": np.where((np.arange(1024)[None, :] - np.arange(128)[:, None]) >= 512,
                            0.0, -50000.0).astype(BF),
        "identb": np.eye(128).astype(BF),
    }
    in_maps = []
    for c in range(NCORES):
        m = dict(shared)
        m["x"] = np.ascontiguousarray(x[c * NSEQ:(c + 1) * NSEQ])
        in_maps.append(m)
    return in_maps


def kernel(**inputs):
    in_maps = _prep_inputs(**inputs)
    nc = build_program()
    res = run_bass_kernel_spmd(nc, in_maps, core_ids=list(range(NCORES)))
    out = np.concatenate([np.asarray(r["y"]) for r in res.results], axis=0)
    return out.astype(np.float32, copy=False)
